# version 01
# baseline (speedup 1.0000x reference)
import numpy as np
import concourse.bass as bass
import concourse.mybir as mybir
from concourse.bass_utils import run_bass_kernel_spmd
from contextlib import ExitStack

F32 = mybir.dt.float32
BF16 = mybir.dt.bfloat16
I32 = mybir.dt.int32
AF = mybir.ActivationFunctionType
ALU = mybir.AluOpType

H = 16
D = 128
HID = 2048
S = 4096
INTER = 5504
NIT = 43
NHC = 16
WIN = 410
RSQ = float(1.0 / np.sqrt(128.0))
BIG = 1.0e9
BISECT_ITERS = 42
HEAVY_K = 409.0


class TSem:
    def __init__(self, h):
        self.h = h
        self.n = 0


class Emit:
    def __init__(self, eng, handles, sems):
        self.eng = eng
        self.h = handles
        self.sems = sems
        self._waited = {}

    def on(self, eng):
        return self.eng == eng

    def wait(self, eng, sem, val=None):
        if val is None:
            val = sem.n
        if val <= 0:
            return
        if self.eng == eng:
            key = id(sem)
            if self._waited.get(key, 0) < val:
                self.h[eng].wait_ge(sem.h, val)
                self._waited[key] = val

    def inc(self, eng, ins, sem, k=1):
        if self.eng == eng and ins is not None:
            ins.then_inc(sem.h, k)
        sem.n += k
        return sem.n




_NON_EMIT = {"wait_ge", "then_inc", "sem_clear", "engine_nop"}


class SelfSync:

    def __init__(self, h, tsem):
        self._h = h
        self._tsem = tsem

    def __getattr__(self, name):
        fn = getattr(self._h, name)
        if name.startswith("_") or name in _NON_EMIT or not callable(fn):
            return fn
        tsem = self._tsem
        h = self._h

        def wrapped(*a, **k):
            if tsem.n > 0:
                h.wait_ge(tsem.h, tsem.n)
            return fn(*a, **k)
        return wrapped


def build_consts():
    c = {}
    inv_freq = (1.0 / (10000.0 ** (np.arange(0, D, 2, dtype=np.float32) / np.float32(D)))).astype(np.float32)
    pos = np.arange(S, dtype=np.float32)
    freqs = pos[:, None] * inv_freq[None, :]
    emb = np.concatenate([freqs, freqs], axis=-1)
    c["cosT"] = np.ascontiguousarray(np.cos(emb).astype(np.float32).T)
    c["sinT"] = np.ascontiguousarray(np.sin(emb).astype(np.float32).T)
    R = np.zeros((D, D), dtype=np.float32)
    for i in range(D):
        if i < 64:
            R[i + 64, i] = -1.0
        else:
            R[i - 64, i] = 1.0
    c["rotC"] = R
    c["identC"] = np.eye(128, dtype=np.float32)
    qi = np.arange(128)[:, None]
    ki = np.arange(128)[None, :]
    c["triC"] = (ki <= qi).astype(np.float32)
    c["iota512C"] = np.broadcast_to(np.arange(512, dtype=np.float32)[None, :], (128, 512)).copy()
    c["iota128cC"] = np.arange(128, dtype=np.float32)[:, None].copy()
    c["onesrC"] = np.ones((1, 128), dtype=np.float32)
    return c


L1_CONSTS = ["cosT", "sinT", "rotC", "identC", "triC", "iota512C", "iota128cC", "onesrC"]


def build_l1():
    nc = bass.Bass("TRN2", target_bir_lowering=False, debug=False, dynamic_dma_scratch_size=4096)
    dt = nc.dram_tensor
    xT = dt("xT", [HID, S], F32, kind="ExternalInput").ap()
    wqT = dt("wqT", [HID, 512], F32, kind="ExternalInput").ap()
    wkT = dt("wkT", [HID, 512], F32, kind="ExternalInput").ap()
    wvT = dt("wvT", [HID, 512], F32, kind="ExternalInput").ap()
    gidxT = dt("gidxT", [HID, 688], I32, kind="ExternalInput").ap()
    uidxT = dt("uidxT", [HID, 688], I32, kind="ExternalInput").ap()
    didxT = dt("didxT", [688, HID], I32, kind="ExternalInput").ap()
    lut16 = dt("lut16", [1, 16], F32, kind="ExternalInput").ap()
    cosT_d = dt("cosT", [D, S], F32, kind="ExternalInput").ap()
    sinT_d = dt("sinT", [D, S], F32, kind="ExternalInput").ap()
    rot_d = dt("rotC", [D, D], F32, kind="ExternalInput").ap()
    ident_d = dt("identC", [128, 128], F32, kind="ExternalInput").ap()
    tri_d = dt("triC", [128, 128], F32, kind="ExternalInput").ap()
    iota512_d = dt("iota512C", [128, 512], F32, kind="ExternalInput").ap()
    iota128c_d = dt("iota128cC", [128, 1], F32, kind="ExternalInput").ap()
    onesr_d = dt("onesrC", [1, 128], F32, kind="ExternalInput").ap()

    attnT_o = dt("attnT_o", [512, S], BF16, kind="ExternalOutput").ap()
    wg_o = dt("wg_o", [HID, 688], BF16, kind="ExternalOutput").ap()
    wu_o = dt("wu_o", [HID, 688], BF16, kind="ExternalOutput").ap()
    wd_o = dt("wd_o", [688, HID], BF16, kind="ExternalOutput").ap()

    tsum_d = dt("tsum_d", [4, S], F32)
    mask_d = dt("mask_d", [4, S], BF16)

    with ExitStack() as ctx:
        e = ctx.enter_context
        cos_s = e(nc.sbuf_tensor("cos_s", [D, S], BF16))
        sin_s = e(nc.sbuf_tensor("sin_s", [D, S], BF16))
        rot_s = e(nc.sbuf_tensor("rot_s", [D, D], BF16))
        identb_s = e(nc.sbuf_tensor("identb_s", [128, 128], BF16))
        tri_s = e(nc.sbuf_tensor("tri_s", [128, 128], BF16))
        iota512_s = e(nc.sbuf_tensor("iota512_s", [128, 512], F32))
        iota128c_s = e(nc.sbuf_tensor("iota128c_s", [128, 1], F32))
        tstage_s = e(nc.sbuf_tensor("tstage_s", [128, 512], F32))
        onesrF_s = e(nc.sbuf_tensor("onesrF_s", [1, 128], F32))
        onescb_s = e(nc.sbuf_tensor("onescb_s", [128, 1], BF16))
        lutrow_s = e(nc.sbuf_tensor("lutrow_s", [1, 16], F32))
        lutb_s = e(nc.sbuf_tensor("lutb_s", [128, 16], F32))
        wq_s = e(nc.sbuf_tensor("wq_s", [128, NHC * 256], BF16))
        wk_s = e(nc.sbuf_tensor("wk_s", [128, NHC * 256], BF16))
        wv_s = e(nc.sbuf_tensor("wv_s", [128, NHC * 256], BF16))
        xf_s = e(nc.sbuf_tensor("xf_s", [128, 8 * 512], F32))
        xb_s = e(nc.sbuf_tensor("xb_s", [128, NHC * 512], BF16))
        qT4 = [e(nc.sbuf_tensor(f"qT4_{i}", [D, S], BF16)) for i in range(4)]
        kT4 = [e(nc.sbuf_tensor(f"kT4_{i}", [D, S], BF16)) for i in range(4)]
        vN4 = [e(nc.sbuf_tensor(f"vN4_{i}", [128, S], BF16)) for i in range(4)]
        rope_t1 = e(nc.sbuf_tensor("rope_t1", [128, 512], F32))
        rope_t2 = e(nc.sbuf_tensor("rope_t2", [128, 512], F32))
        qsb_s = e(nc.sbuf_tensor("qsb_s", [128, 512], BF16))
        pA_s = [e(nc.sbuf_tensor(f"pA{i}_s", [128, S], BF16)) for i in range(2)]
        zpart_s = e(nc.sbuf_tensor("zpart_s", [128, 12], F32))
        zsum_s = e(nc.sbuf_tensor("zsum_s", [128, 1], F32))
        rinv_s = e(nc.sbuf_tensor("rinv_s", [128, 1], F32))
        rinvb_s = e(nc.sbuf_tensor("rinvb_s", [128, 1], BF16))
        bis_scrap = e(nc.sbuf_tensor("bis_scrap", [128, S // 2], BF16))
        bis_lo = e(nc.sbuf_tensor("bis_lo", [128, 1], F32))
        bis_hi = e(nc.sbuf_tensor("bis_hi", [128, 1], F32))
        bis_mid = e(nc.sbuf_tensor("bis_mid", [128, 1], F32))
        bis_cnt = e(nc.sbuf_tensor("bis_cnt", [128, 1], F32))
        bis_pred = e(nc.sbuf_tensor("bis_pred", [128, 1], F32))
        bis_d = e(nc.sbuf_tensor("bis_d", [128, 1], F32))
        bis_t = e(nc.sbuf_tensor("bis_t", [128, 1], F32))
        m32_s = e(nc.sbuf_tensor("m32_s", [128, 128], BF16))
        maskT_s = e(nc.sbuf_tensor("maskT_s", [128, 128], F32))
        pcP_s = [e(nc.sbuf_tensor(f"pcP{i}_s", [128, 512], BF16)) for i in range(2)]
        pcS_s = e(nc.sbuf_tensor("pcS_s", [128, 1], F32))
        pcS410_s = e(nc.sbuf_tensor("pcS410_s", [128, 1], F32))
        pcE_s = e(nc.sbuf_tensor("pcE_s", [128, 1], F32))
        pcr_s = e(nc.sbuf_tensor("pcr_s", [1, 512], F32))
        pcrB_s = e(nc.sbuf_tensor("pcrB_s", [128, 512], F32))
        pcout_s = [e(nc.sbuf_tensor(f"pcout{i}_s", [128, 512], BF16)) for i in range(2)]
        dec_idx = [e(nc.sbuf_tensor(f"dec_idx{i}", [128, 688], BF16)) for i in range(2)]
        dec_acc = [e(nc.sbuf_tensor(f"dec_acc{i}", [128, 688], BF16)) for i in range(2)]
        dec_tmp = e(nc.sbuf_tensor("dec_tmp", [128, 688], BF16))

        ps = [e(nc.psum_tensor(f"ps{i}", [128, 512], F32)) for i in range(6)]
        p_trb = e(nc.psum_tensor("p_trb", [128, 128], BF16))

        sems = {n: TSem(e(nc.semaphore(n)))
                for n in ["s_ld", "s_x", "s_mm", "s_act", "s_dve", "s_st", "s_dk", "s_gp", "s_gm"]}
        block = e(nc.Block())

        state = {}

        def schedule(em: Emit):
            state.clear()
            SM = sems
            t = em.h.get("t")
            v = em.h.get("v")
            s = em.h.get("s")
            g = em.h.get("g")
            q = em.h.get("q")

            for dst, src in [(iota128c_s, iota128c_d), (lutrow_s, lut16), (onesrF_s, onesr_d)]:
                ins = q.dma_start(out=dst[:, :], in_=src) if em.on("q") else None
                em.inc("q", ins, SM["s_ld"], 16)
            ins = q.dma_start(out=iota512_s[:, :], in_=iota512_d) if em.on("q") else None
            em.inc("q", ins, SM["s_ld"], 16)
            for dst, src, nrow, ncol in [(rot_s, rot_d, D, D), (identb_s, ident_d, 128, 128),
                                         (tri_s, tri_d, 128, 128)]:
                ins = q.dma_start(out=tstage_s[0:nrow, 0:ncol], in_=src) if em.on("q") else None
                lv = em.inc("q", ins, SM["s_ld"], 16)
                em.wait("v", SM["s_ld"], lv)
                ins = v.tensor_copy(dst[0:nrow, 0:ncol], tstage_s[0:nrow, 0:ncol]) if em.on("v") else None
                dvv = em.inc("v", ins, SM["s_dve"])
                em.wait("q", SM["s_dve"], dvv)
            for dst, src in [(cos_s, cosT_d), (sin_s, sinT_d)]:
                for cc in range(0, S, 512):
                    ins = q.dma_start(out=tstage_s[:, :], in_=src[:, cc:cc + 512]) if em.on("q") else None
                    lv = em.inc("q", ins, SM["s_ld"], 16)
                    em.wait("v", SM["s_ld"], lv)
                    ins = v.tensor_copy(dst[:, cc:cc + 512], tstage_s[:, :]) if em.on("v") else None
                    dvv = em.inc("v", ins, SM["s_dve"])
                    em.wait("q", SM["s_dve"], dvv)
            ins = v.memset(onescb_s[:, :], 1.0) if em.on("v") else None
            em.inc("v", ins, SM["s_dve"])
            ins = v.memset(m32_s[:, :], 0.0) if em.on("v") else None
            em.inc("v", ins, SM["s_dve"])
            em.wait("t", SM["s_ld"])
            ins = t.matmul(ps[5][:, 0:16], onesrF_s[:, :], lutrow_s[:, :], start=True, stop=True) if em.on("t") else None
            em.inc("t", ins, SM["s_mm"])
            em.wait("s", SM["s_mm"])
            ins = s.copy(lutb_s[:, :], ps[5][:, 0:16]) if em.on("s") else None
            lut_ready = em.inc("s", ins, SM["s_act"])

            dec_jobs = []
            for hc in range(NHC):
                dec_jobs.append((gidxT, wg_o, hc * 128, 128, 0, 688))
            for hc in range(NHC):
                dec_jobs.append((uidxT, wu_o, hc * 128, 128, 0, 688))
            for it in range(6):
                r0 = it * 128
                rows = min(128, 688 - r0)
                for c0, cols in [(0, 688), (688, 688), (1376, 672)]:
                    dec_jobs.append((didxT, wd_o, r0, rows, c0, cols))
            dec_buf_free = [0, 0]

            def decode_ops():
                for j, (srcd, dstd, r0, rows, c0, cols) in enumerate(dec_jobs):
                    buf = dec_idx[j % 2]
                    acc = dec_acc[j % 2]

                    def op_load(j=j, srcd=srcd, r0=r0, rows=rows, c0=c0, cols=cols, buf=buf):
                        em.wait("g", SM["s_st"])
                        ins = g.dma_start(out=buf[0:rows, 0:cols],
                                          in_=srcd[r0:r0 + rows, c0:c0 + cols]) if em.on("g") else None
                        lv = em.inc("g", ins, SM["s_dk"], 16)
                        state[("dk", j)] = lv
                    yield op_load

                    def op_first(j=j, rows=rows, cols=cols, buf=buf, acc=acc):
                        em.wait("v", SM["s_dk"], state[("dk", j)])
                        em.wait("v", SM["s_act"], lut_ready)
                        em.wait("v", SM["s_st"])
                        ins = v.tensor_scalar(acc[0:rows, 0:cols], buf[0:rows, 0:cols], 0.0,
                                              lutb_s[0:rows, 0:1], ALU.is_equal, ALU.mult) if em.on("v") else None
                        em.inc("v", ins, SM["s_dve"])
                    yield op_first

                    for tv in range(1, 16):
                        def op_sel(tv=tv, rows=rows, cols=cols, buf=buf):
                            ins = v.tensor_scalar(dec_tmp[0:rows, 0:cols], buf[0:rows, 0:cols], float(tv),
                                                  lutb_s[0:rows, tv:tv + 1], ALU.is_equal, ALU.mult) if em.on("v") else None
                            em.inc("v", ins, SM["s_dve"])
                        yield op_sel

                        def op_add(rows=rows, cols=cols, acc=acc):
                            ins = v.tensor_add(acc[0:rows, 0:cols], acc[0:rows, 0:cols],
                                               dec_tmp[0:rows, 0:cols]) if em.on("v") else None
                            em.inc("v", ins, SM["s_dve"])
                        yield op_add

                    def op_store(j=j, dstd=dstd, r0=r0, rows=rows, c0=c0, cols=cols, acc=acc):
                        em.wait("g", SM["s_dve"])
                        ins = g.dma_start(out=dstd[r0:r0 + rows, c0:c0 + cols],
                                          in_=acc[0:rows, 0:cols]) if em.on("g") else None
                        sv = em.inc("g", ins, SM["s_st"], 16)
                        dec_buf_free[j % 2] = sv
                    yield op_store

            dec_stream = decode_ops()
            dec_done = [False]

            def emit_decode(n):
                for _ in range(n):
                    if dec_done[0]:
                        return
                    op = next(dec_stream, None)
                    if op is None:
                        dec_done[0] = True
                        return
                    op()

            import os as _os2
            _NPAIR = 2 if int(_os2.environ.get("L1_STAGES", "9")) >= 2 else 0
            _SUB = int(_os2.environ.get("L1_SUB", "9"))
            for pair in range(_NPAIR):
                for wsb, wsrc, scale in [(wq_s, wqT, RSQ), (wk_s, wkT, 1.0), (wv_s, wvT, 1.0)]:
                    for half in range(2):
                        src3 = wsrc[:, pair * 256:(pair + 1) * 256].rearrange(
                            "(hc p) o -> p hc o", p=128)[:, half * 8:(half + 1) * 8, :]
                        dst3 = xf_s[:, 0:2048].rearrange("p (hc o) -> p hc o", o=256)
                        ins = q.dma_start(out=dst3, in_=src3) if em.on("q") else None
                        lv = em.inc("q", ins, SM["s_ld"], 16)
                        em.wait("s", SM["s_ld"], lv)
                        dstw = wsb[:, half * 8 * 256:(half + 1) * 8 * 256]
                        ins = s.activation(dstw, xf_s[:, 0:2048], AF.Copy, scale=float(scale)) if em.on("s") else None
                        wdone = em.inc("s", ins, SM["s_act"])
                        em.wait("q", SM["s_act"], wdone)
                if _SUB < 2:
                    continue
                for tc in range(8):
                    for half in range(2):
                        src3 = xT[:, tc * 512:(tc + 1) * 512].rearrange(
                            "(hc p) t -> p hc t", p=128)[:, half * 8:(half + 1) * 8, :]
                        ins = q.dma_start(out=xf_s[:, 0:4096].rearrange("p (hc t) -> p hc t", t=512),
                                          in_=src3) if em.on("q") else None
                        lv = em.inc("q", ins, SM["s_x"], 16)
                        em.wait("v", SM["s_x"], lv)
                        em.wait("v", SM["s_mm"])
                        dstb = xb_s[:, half * 8 * 512:(half + 1) * 8 * 512]
                        ins = v.tensor_copy(dstb, xf_s[:, 0:4096]) if em.on("v") else None
                        cd = em.inc("v", ins, SM["s_dve"])
                        em.wait("q", SM["s_dve"], cd)
                    xb_ready = SM["s_dve"].n
                    em.wait("t", SM["s_dve"], xb_ready)
                    for hh in range(2 if _SUB >= 3 else 0):
                        slot = pair * 2 + hh
                        for wi, (wsb, dstT) in enumerate([(wq_s, qT4), (wk_s, kT4)]):
                            pp = ps[wi]
                            ins = None
                            for hc in range(NHC):
                                ins = t.matmul(pp[:, :],
                                               wsb[:, hc * 256 + hh * 128: hc * 256 + hh * 128 + 128],
                                               xb_s[:, hc * 512:(hc + 1) * 512],
                                               start=(hc == 0), stop=(hc == NHC - 1)) if em.on("t") else None
                            mm_done = em.inc("t", ins, SM["s_mm"])
                            if _SUB < 4:
                                continue
                            em.wait("s", SM["s_mm"], mm_done)
                            em.wait("s", SM["s_mm"], state.get("qsb_free", 0))
                            ins = s.copy(qsb_s[:, :], pp[:, :]) if em.on("s") else None
                            ev = em.inc("s", ins, SM["s_act"])
                            em.wait("t", SM["s_act"], ev)
                            ins = t.matmul(ps[3][:, :], rot_s[:, :], qsb_s[:, :],
                                           start=True, stop=True) if em.on("t") else None
                            rot_done = em.inc("t", ins, SM["s_mm"])
                            state["qsb_free"] = rot_done
                            em.wait("v", SM["s_mm"], rot_done)
                            ins = v.tensor_mul(rope_t1[:, :], ps[3][:, :],
                                               sin_s[:, tc * 512:(tc + 1) * 512]) if em.on("v") else None
                            em.inc("v", ins, SM["s_dve"])
                            ins = v.tensor_mul(rope_t2[:, :], pp[:, :],
                                               cos_s[:, tc * 512:(tc + 1) * 512]) if em.on("v") else None
                            em.inc("v", ins, SM["s_dve"])
                            ins = v.tensor_add(dstT[slot][:, tc * 512:(tc + 1) * 512],
                                               rope_t1[:, :], rope_t2[:, :]) if em.on("v") else None
                            rope_done = em.inc("v", ins, SM["s_dve"])
                            em.wait("t", SM["s_dve"], rope_done)
                    for tt in range(4 if _SUB >= 5 else 0):
                        vbank = 2 if tt % 2 == 0 else 4
                        em.wait("t", SM["s_act"], state.get(("vfree", vbank), 0))
                        ins = None
                        for hh in range(2):
                            for hc in range(NHC):
                                ins = t.matmul(ps[vbank][:, hh * 128: hh * 128 + 128],
                                               xb_s[:, hc * 512 + tt * 128: hc * 512 + tt * 128 + 128],
                                               wv_s[:, hc * 256 + hh * 128: hc * 256 + hh * 128 + 128],
                                               start=(hc == 0), stop=(hc == NHC - 1)) if em.on("t") else None
                        vm = em.inc("t", ins, SM["s_mm"])
                        em.wait("s", SM["s_mm"], vm)
                        ins = None
                        for hh in range(2):
                            slot = pair * 2 + hh
                            ins = s.copy(vN4[slot][:, (tc * 4 + tt) * 128:(tc * 4 + tt) * 128 + 128],
                                         ps[vbank][:, hh * 128: hh * 128 + 128]) if em.on("s") else None
                        vf = em.inc("s", ins, SM["s_act"])
                        state[("vfree", vbank)] = vf
                    emit_decode(6)

            pa_free = {0: 0, 1: 0}
            cs_evac_done = [0]

            def pass_a(hd):
                em.wait("t", SM["s_dve"])
                b = None
                for qt in range(32):
                    b = qt % 2
                    kmax = (qt + 1) * 128
                    pieces = []
                    cpos = 0
                    while cpos < kmax - 128:
                        w = min(512, kmax - 128 - cpos)
                        pieces.append((cpos, w, False))
                        cpos += w
                    pieces.append((kmax - 128, 128, True))
                    em.wait("s", SM["s_mm"], pa_free[b])
                    npc = len(pieces)
                    for pi, (c0, w, is_diag) in enumerate(pieces):
                        bank = ps[pi % 2]
                        em.wait("t", SM["s_act"], state.get(("scfree", pi % 2), 0))
                        ins = t.matmul(bank[:, 0:w], qT4[hd][:, qt * 128:(qt + 1) * 128],
                                       kT4[hd][:, c0:c0 + w], start=True, stop=True) if em.on("t") else None
                        smm = em.inc("t", ins, SM["s_mm"])
                        em.wait("s", SM["s_mm"], smm)
                        if not is_diag:
                            ins = s.activation(pA_s[b][:, c0:c0 + w], bank[:, 0:w], AF.Exp,
                                               accum_out=zpart_s[:, pi:pi + 1]) if em.on("s") else None
                            sa = em.inc("s", ins, SM["s_act"])
                        else:
                            ins = s.activation(pA_s[b][:, c0:c0 + w], bank[:, 0:w], AF.Exp) if em.on("s") else None
                            sa = em.inc("s", ins, SM["s_act"])
                            em.wait("v", SM["s_act"], sa)
                            ins = v.scalar_tensor_tensor(pA_s[b][:, c0:c0 + w], tri_s[:, :], 1.0,
                                                         pA_s[b][:, c0:c0 + w], ALU.mult, ALU.mult,
                                                         accum_out=zpart_s[:, pi:pi + 1]) if em.on("v") else None
                            em.inc("v", ins, SM["s_dve"])
                        state[("scfree", pi % 2)] = sa
                    em.wait("v", SM["s_act"])
                    ins = v.reduce_sum(zsum_s[:, 0:1], zpart_s[:, 0:npc], axis=mybir.AxisListType.X) if em.on("v") else None
                    em.inc("v", ins, SM["s_dve"])
                    ins = v.reciprocal(rinv_s[:, 0:1], zsum_s[:, 0:1]) if em.on("v") else None
                    em.inc("v", ins, SM["s_dve"])
                    ins = v.tensor_copy(rinvb_s[:, 0:1], rinv_s[:, 0:1]) if em.on("v") else None
                    rv = em.inc("v", ins, SM["s_dve"])
                    kpad = ((kmax + 511) // 512) * 512
                    if kpad > kmax:
                        em.wait("v", SM["s_mm"], pa_free[b])
                        ins = v.memset(pA_s[b][:, kmax:kpad], 0.0) if em.on("v") else None
                        rv = em.inc("v", ins, SM["s_dve"])
                    em.wait("t", SM["s_dve"], rv)
                    ins = None
                    for ck in range(kpad // 512):
                        first = (qt == 4 * ck)
                        ins = t.matmul(ps[2 + (ck % 3)][32 * (ck // 3):32 * (ck // 3) + 1, 0:512],
                                       rinvb_s[:, 0:1], pA_s[b][:, 512 * ck:512 * ck + 512],
                                       start=first, stop=(qt == 31)) if em.on("t") else None
                    pa_free[b] = em.inc("t", ins, SM["s_mm"])
                    if qt % 4 == 3:
                        emit_decode(5)
                em.wait("v", SM["s_mm"], pa_free[b])
                for ck in range(8):
                    r0 = 32 * (ck // 3)
                    em.wait("v", SM["s_gp"])
                    ins = v.tensor_copy(tstage_s[r0:r0 + 1, 0:512],
                                        ps[2 + (ck % 3)][r0:r0 + 1, 0:512]) if em.on("v") else None
                    dvv = em.inc("v", ins, SM["s_dve"])
                    em.wait("q", SM["s_dve"], dvv)
                    ins = q.dma_start(out=tsum_d[hd:hd + 1, 512 * ck:512 * ck + 512],
                                      in_=tstage_s[r0:r0 + 1, 0:512]) if em.on("q") else None
                    em.inc("q", ins, SM["s_gp"], 16)
                cs_evac_done[0] = SM["s_dve"].n

            def bisect_begin(p):
                em.wait("q", SM["s_gp"])
                em.wait("q", SM["s_dve"])
                ins = q.dma_start(out=xf_s[0:1, 0:S], in_=tsum_d[2 * p:2 * p + 1, :]) if em.on("q") else None
                em.inc("q", ins, SM["s_x"], 16)
                ins = q.dma_start(out=xf_s[64:65, 0:S], in_=tsum_d[2 * p + 1:2 * p + 2, :]) if em.on("q") else None
                lv = em.inc("q", ins, SM["s_x"], 16)
                em.wait("v", SM["s_x"], lv)
                ins = v.memset(bis_lo[:, :], 0.0) if em.on("v") else None
                em.inc("v", ins, SM["s_dve"])
                ins = v.memset(bis_hi[:, :], 4096.0) if em.on("v") else None
                em.inc("v", ins, SM["s_dve"])

            def bisect_iter():
                ins = v.tensor_add(bis_mid[:, :], bis_lo[:, :], bis_hi[:, :]) if em.on("v") else None
                em.inc("v", ins, SM["s_dve"])
                ins = v.tensor_scalar(bis_mid[:, :], bis_mid[:, :], 0.5, 0.0, ALU.mult, ALU.add) if em.on("v") else None
                em.inc("v", ins, SM["s_dve"])
                ins = v.tensor_scalar(bis_scrap[:, 0:S // 2], xf_s[:, 0:S // 2], bis_mid[:, 0:1], 0.0,
                                      ALU.is_gt, ALU.add, accum_out=bis_cnt[:, 0:1]) if em.on("v") else None
                em.inc("v", ins, SM["s_dve"])
                ins = v.tensor_scalar(bis_scrap[:, 0:S // 2], xf_s[:, S // 2:S], bis_mid[:, 0:1], 0.0,
                                      ALU.is_gt, ALU.add, accum_out=bis_t[:, 0:1]) if em.on("v") else None
                em.inc("v", ins, SM["s_dve"])
                ins = v.tensor_add(bis_cnt[:, :], bis_cnt[:, :], bis_t[:, :]) if em.on("v") else None
                em.inc("v", ins, SM["s_dve"])
                ins = v.tensor_scalar(bis_pred[:, :], bis_cnt[:, :], HEAVY_K, 1.0,
                                      ALU.is_ge, ALU.mult) if em.on("v") else None
                em.inc("v", ins, SM["s_dve"])
                ins = v.tensor_sub(bis_d[:, :], bis_mid[:, :], bis_lo[:, :]) if em.on("v") else None
                em.inc("v", ins, SM["s_dve"])
                ins = v.tensor_mul(bis_t[:, :], bis_pred[:, :], bis_d[:, :]) if em.on("v") else None
                em.inc("v", ins, SM["s_dve"])
                ins = v.tensor_add(bis_lo[:, :], bis_lo[:, :], bis_t[:, :]) if em.on("v") else None
                em.inc("v", ins, SM["s_dve"])
                ins = v.tensor_scalar(bis_d[:, :], bis_pred[:, :], -1.0, 1.0, ALU.mult, ALU.add) if em.on("v") else None
                em.inc("v", ins, SM["s_dve"])
                ins = v.tensor_sub(bis_t[:, :], bis_mid[:, :], bis_hi[:, :]) if em.on("v") else None
                em.inc("v", ins, SM["s_dve"])
                ins = v.tensor_mul(bis_t[:, :], bis_d[:, :], bis_t[:, :]) if em.on("v") else None
                em.inc("v", ins, SM["s_dve"])
                ins = v.tensor_add(bis_hi[:, :], bis_hi[:, :], bis_t[:, :]) if em.on("v") else None
                em.inc("v", ins, SM["s_dve"])

            def bisect_end(p):
                em.wait("v", SM["s_gm"])
                gv = 0
                for hf in range(2):
                    ins = v.tensor_scalar(bis_scrap[:, 0:S // 2], xf_s[:, hf * (S // 2):(hf + 1) * (S // 2)],
                                          bis_lo[:, 0:1], 1.0,
                                          ALU.is_gt, ALU.mult) if em.on("v") else None
                    mk = em.inc("v", ins, SM["s_dve"])
                    em.wait("g", SM["s_dve"], mk)
                    ins = g.dma_start(out=mask_d[2 * p:2 * p + 1, hf * (S // 2):(hf + 1) * (S // 2)],
                                      in_=bis_scrap[0:1, 0:S // 2]) if em.on("g") else None
                    em.inc("g", ins, SM["s_gm"], 16)
                    ins = g.dma_start(out=mask_d[2 * p + 1:2 * p + 2, hf * (S // 2):(hf + 1) * (S // 2)],
                                      in_=bis_scrap[64:65, 0:S // 2]) if em.on("g") else None
                    gv = em.inc("g", ins, SM["s_gm"], 16)
                    em.wait("v", SM["s_gm"], gv)
                em.wait("q", SM["s_gm"], gv)
                lv = 0
                for hh in range(2):
                    hd = 2 * p + hh
                    src = mask_d[hd:hd + 1, :].rearrange("one (c p) -> (one c) p", p=128)
                    ins = q.dma_start(out=m32_s[32 * hd:32 * hd + 32, :], in_=src) if em.on("q") else None
                    lv = em.inc("q", ins, SM["s_x"], 16)
                em.wait("t", SM["s_x"], lv)
                em.wait("t", SM["s_act"], state.get("p_trb_free", 0))
                ins = t.transpose(p_trb[:, :], m32_s[:, :], identb_s[:, :]) if em.on("t") else None
                tv2 = em.inc("t", ins, SM["s_mm"])
                em.wait("s", SM["s_mm"], tv2)
                ins = s.copy(maskT_s[:, :], p_trb[:, :]) if em.on("s") else None
                mt = em.inc("s", ins, SM["s_act"])
                state[("maskT", p)] = mt
                state["p_trb_free"] = mt

            bis_state = {}

            def bisect_step(p, budget):
                st_ = bis_state.setdefault(p, {"phase": 0, "it": 0})
                if st_["phase"] == 0:
                    bisect_begin(p)
                    st_["phase"] = 1
                while budget > 0 and st_["phase"] == 1:
                    bisect_iter()
                    st_["it"] += 1
                    budget -= 1
                    if st_["it"] >= BISECT_ITERS:
                        st_["phase"] = 2
                        bisect_end(p)

            def bisect_finish(p):
                st_ = bis_state.setdefault(p, {"phase": 0, "it": 0})
                if st_["phase"] == 0:
                    bisect_begin(p)
                    st_["phase"] = 1
                while st_["phase"] == 1:
                    bisect_iter()
                    st_["it"] += 1
                    if st_["it"] >= BISECT_ITERS:
                        st_["phase"] = 2
                        bisect_end(p)

            av_free = {0: 0, 1: 0}
            z_free = {0: 0, 1: 0}
            out_free = {0: 0, 1: 0}

            def pass_c(hd, interleave=None):
                em.wait("t", SM["s_act"], state[("maskT", hd // 2)])
                for qc in range(8):
                    par = qc % 2
                    nkt = 4 * qc + 4
                    em.wait("t", SM["s_dve"], av_free[par])
                    em.wait("t", SM["s_dve"], z_free[par])
                    for kt in range(nkt):
                        bank = kt % 2
                        em.wait("t", SM["s_act"], state.get(("pcsc", bank), 0))
                        ins = t.matmul(ps[bank][:, 0:512], kT4[hd][:, kt * 128:(kt + 1) * 128],
                                       qT4[hd][:, qc * 512:(qc + 1) * 512], start=True, stop=True) if em.on("t") else None
                        smm = em.inc("t", ins, SM["s_mm"])
                        em.wait("s", SM["s_mm"], smm)
                        em.wait("s", SM["s_mm"], state.get(("pcPfree", bank), 0))
                        ins = s.activation(pcP_s[bank][:, :], ps[bank][:, 0:512], AF.Exp) if em.on("s") else None
                        sa = em.inc("s", ins, SM["s_act"])
                        state[("pcsc", bank)] = sa
                        em.wait("v", SM["s_act"], sa)
                        col = maskT_s[:, 32 * hd + kt:32 * hd + kt + 1]
                        if 128 * kt + 127 + 409 < 512 * qc:
                            ins = v.tensor_scalar(pcP_s[bank][:, :], pcP_s[bank][:, :], col, 0.0,
                                                  ALU.mult, ALU.add) if em.on("v") else None
                            dv2 = em.inc("v", ins, SM["s_dve"])
                        else:
                            ins = v.tensor_scalar(pcS_s[:, :], iota128c_s[:, :],
                                                  float(128 * kt - 512 * qc), 0.0,
                                                  ALU.add, ALU.add) if em.on("v") else None
                            em.inc("v", ins, SM["s_dve"])
                            ins = v.tensor_scalar(pcS410_s[:, :], pcS_s[:, :], float(WIN), 0.0,
                                                  ALU.add, ALU.add) if em.on("v") else None
                            em.inc("v", ins, SM["s_dve"])
                            ins = v.scalar_tensor_tensor(pcE_s[:, :], col, BIG, pcS410_s[:, :],
                                                         ALU.mult, ALU.add) if em.on("v") else None
                            em.inc("v", ins, SM["s_dve"])
                            ins = v.scalar_tensor_tensor(pcP_s[bank][:, :], iota512_s[:, :],
                                                         pcS_s[:, 0:1], pcP_s[bank][:, :],
                                                         ALU.is_ge, ALU.mult) if em.on("v") else None
                            em.inc("v", ins, SM["s_dve"])
                            ins = v.scalar_tensor_tensor(pcP_s[bank][:, :], iota512_s[:, :],
                                                         pcE_s[:, 0:1], pcP_s[bank][:, :],
                                                         ALU.is_lt, ALU.mult) if em.on("v") else None
                            dv2 = em.inc("v", ins, SM["s_dve"])
                        em.wait("t", SM["s_dve"], dv2)
                        ins = t.matmul(ps[4 + par][0:1, 0:512], onescb_s[:, 0:1], pcP_s[bank][:, :],
                                       start=(kt == 0), stop=(kt == nkt - 1)) if em.on("t") else None
                        em.inc("t", ins, SM["s_mm"])
                        ins = t.matmul(ps[2 + par][:, 0:512], vN4[hd][:, kt * 128:(kt + 1) * 128],
                                       pcP_s[bank][:, :], start=(kt == 0), stop=(kt == nkt - 1)) if em.on("t") else None
                        avm = em.inc("t", ins, SM["s_mm"])
                        state[("pcPfree", bank)] = avm
                    em.wait("v", SM["s_mm"])
                    ins = v.reciprocal(pcr_s[0:1, :], ps[4 + par][0:1, 0:512]) if em.on("v") else None
                    zf = em.inc("v", ins, SM["s_dve"])
                    z_free[par] = zf
                    em.wait("t", SM["s_dve"], zf)
                    em.wait("t", SM["s_act"], state.get(("pcsc", 0), 0))
                    ins = t.matmul(ps[0][:, 0:512], onesrF_s[:, :], pcr_s[0:1, :],
                                   start=True, stop=True) if em.on("t") else None
                    bm = em.inc("t", ins, SM["s_mm"])
                    em.wait("s", SM["s_mm"], bm)
                    ins = s.copy(pcrB_s[:, :], ps[0][:, 0:512]) if em.on("s") else None
                    bca = em.inc("s", ins, SM["s_act"])
                    state[("pcsc", 0)] = bca
                    em.wait("v", SM["s_act"], bca)
                    em.wait("v", SM["s_gm"])
                    ins = v.tensor_mul(pcout_s[par][:, :], ps[2 + par][:, 0:512], pcrB_s[:, :]) if em.on("v") else None
                    av_free[par] = em.inc("v", ins, SM["s_dve"])
                    em.wait("g", SM["s_dve"], av_free[par])
                    ins = g.dma_start(out=attnT_o[hd * 128:(hd + 1) * 128, qc * 512:(qc + 1) * 512],
                                      in_=pcout_s[par][:, :]) if em.on("g") else None
                    out_free[par] = em.inc("g", ins, SM["s_gm"], 16)
                    if interleave is not None:
                        interleave(qc)
                    emit_decode(4)

            import os as _os
            _STG = int(_os.environ.get("L1_STAGES", "9"))
            if _STG >= 3:
                pass_a(0)
                pass_a(1)
            if _STG >= 4:
                bisect_step(0, 10)
                pass_a(2)
                bisect_step(0, BISECT_ITERS)
                bisect_finish(0)
                pass_a(3)
            if _STG >= 5:
                pass_c(0, interleave=lambda qc: bisect_step(1, 3))
                bisect_finish(1)
                pass_c(1)
            if _STG >= 6:
                pass_c(2)
                pass_c(3)
            if _STG >= 2 or True:
                pass
            emit_decode(10 ** 9 if _STG >= 1 else 0)
            _dead = '''
            pass_a(0)
            '''

            if _os.environ.get("L1_DEBUG_DUMP") == "2":
                em.wait("g", SM["s_dve"])
                em.wait("g", SM["s_mm"])
                em.wait("g", SM["s_act"])
                em.wait("g", SM["s_gp"])
                em.wait("g", SM["s_gm"])
                ins = g.dma_start(out=attnT_o[0:4, :], in_=tsum_d[0:4, :]) if em.on("g") else None
                em.inc("g", ins, SM["s_gm"], 16)
                ins = g.dma_start(out=attnT_o[4:6, :], in_=mask_d[0:2, :]) if em.on("g") else None
                em.inc("g", ins, SM["s_gm"], 16)
                ins = g.dma_start(out=attnT_o[128:256, 0:128], in_=maskT_s[:, :]) if em.on("g") else None
                em.inc("g", ins, SM["s_gm"], 16)
                ins = g.dma_start(out=attnT_o[256:384, :], in_=pA_s[0][:, :]) if em.on("g") else None
                em.inc("g", ins, SM["s_gm"], 16)
            elif _os.environ.get("L1_DEBUG_DUMP"):
                em.wait("g", SM["s_dve"])
                em.wait("g", SM["s_mm"])
                em.wait("g", SM["s_act"])
                for rowblk, srcdbg in [(0, qT4[0]), (1, kT4[0]), (2, vN4[0]), (3, pA_s[0])]:
                    ins = g.dma_start(out=attnT_o[rowblk * 128:(rowblk + 1) * 128, :],
                                      in_=srcdbg[:, 0:S]) if em.on("g") else None
                    em.inc("g", ins, SM["s_gm"], 16)
            em.wait("g", SM["s_st"])
            em.wait("g", SM["s_gm"])
            em.wait("q", SM["s_gp"])
            em.wait("q", SM["s_gm"])

        handles_all = {}

        @block.tensor
        def _(tt):
            handles_all["t"] = tt
            schedule(Emit("t", {"t": tt, "v": None, "s": None, "g": None, "q": None}, sems))

        for sm in sems.values():
            sm.n = 0

        @block.vector
        def _(vv):
            schedule(Emit("v", {"t": None, "v": SelfSync(vv, sems["s_dve"]), "s": None, "g": None, "q": None}, sems))

        for sm in sems.values():
            sm.n = 0

        @block.scalar
        def _(ss):
            schedule(Emit("s", {"t": None, "v": None, "s": SelfSync(ss, sems["s_act"]), "g": None, "q": None}, sems))

        for sm in sems.values():
            sm.n = 0

        @block.gpsimd
        def _(gg):
            schedule(Emit("g", {"t": None, "v": None, "s": None, "g": gg, "q": None}, sems))

        for sm in sems.values():
            sm.n = 0

        @block.sync
        def _(qq):
            schedule(Emit("q", {"t": None, "v": None, "s": None, "g": None, "q": qq}, sems))

    return nc


def build_l2():
    nc = bass.Bass("TRN2", target_bir_lowering=False, debug=False, dynamic_dma_scratch_size=4096)
    dt = nc.dram_tensor
    attnT = dt("attnT", [HID, 1024], BF16, kind="ExternalInput").ap()
    xTs = dt("xTs", [HID, 1024], F32, kind="ExternalInput").ap()
    woT = dt("woT", [HID, HID], F32, kind="ExternalInput").ap()
    wgT = dt("wgT", [HID, INTER], BF16, kind="ExternalInput").ap()
    wuT = dt("wuT", [HID, INTER], BF16, kind="ExternalInput").ap()
    wdT = dt("wdT", [INTER, HID], BF16, kind="ExternalInput").ap()
    onesr_d = dt("onesrC", [1, 128], F32, kind="ExternalInput").ap()
    outT = dt("outT", [HID, 1024], F32, kind="ExternalOutput").ap()

    EPS = 1.0e-5

    with ExitStack() as ctx:
        e = ctx.enter_context
        onesrF_s = e(nc.sbuf_tensor("onesrF_s", [1, 128], F32))
        onescb_s = e(nc.sbuf_tensor("onescb_s", [128, 1], BF16))
        at_s = e(nc.sbuf_tensor("at_s", [128, NHC * 512], BF16))
        scrf = [e(nc.sbuf_tensor(f"scrf{i}", [128, 512], F32)) for i in range(2)]
        scrb = e(nc.sbuf_tensor("scrb", [128, 11264], BF16))
        xf2 = [e(nc.sbuf_tensor(f"xf2_{i}", [128, 512], F32)) for i in range(2)]
        acc_s = e(nc.sbuf_tensor("acc_s", [128, NHC * 512], F32))
        xfb_s = e(nc.sbuf_tensor("xfb_s", [128, NHC * 1024], BF16))
        hh_s = e(nc.sbuf_tensor("hh_s", [128, NIT * 1024], BF16))
        lnb_s = e(nc.sbuf_tensor("lnb_s", [128, 512], BF16))
        ln2b_s = e(nc.sbuf_tensor("ln2b_s", [128, 512], BF16))
        sg_s = e(nc.sbuf_tensor("sg_s", [128, 512], BF16))
        mean_s = e(nc.sbuf_tensor("mean_s", [1, 512], F32))
        ex2_s = e(nc.sbuf_tensor("ex2_s", [1, 512], F32))
        var_s = e(nc.sbuf_tensor("var_s", [1, 512], F32))
        sq_s = e(nc.sbuf_tensor("sq_s", [1, 512], F32))
        rstd_s = e(nc.sbuf_tensor("rstd_s", [1, 512], F32))
        meanB_s = e(nc.sbuf_tensor("meanB_s", [128, 512], F32))
        rstdB_s = e(nc.sbuf_tensor("rstdB_s", [128, 512], F32))
        tmp_s = e(nc.sbuf_tensor("tmp_s", [128, 512], F32))

        ps = [e(nc.psum_tensor(f"ps{i}", [128, 512], F32)) for i in range(6)]

        sems = {n: TSem(e(nc.semaphore(n)))
                for n in ["s_ld", "s_x", "s_mm", "s_act", "s_dve", "s_st",
                          "s_w0", "s_w1", "s_x0", "s_x1"]}
        block = e(nc.Block())

        state = {}

        def layernorm(em, tc2, src_f32, dst_bf, dst_cols_per_ot, t_off):
            SM = sems
            t = em.h.get("t")
            v = em.h.get("v")
            s = em.h.get("s")
            ins = None
            for ot in range(NHC):
                sl = src_f32[:, ot * 512:(ot + 1) * 512]
                ins = v.tensor_copy(lnb_s[:, :], sl) if em.on("v") else None
                em.inc("v", ins, SM["s_dve"])
                em.wait("s", SM["s_dve"])
                ins = s.activation(ln2b_s[:, :], sl, AF.Square) if em.on("s") else None
                sa = em.inc("s", ins, SM["s_act"])
                dvv = SM["s_dve"].n
                em.wait("t", SM["s_dve"], dvv)
                em.wait("t", SM["s_act"], sa)
                ins = t.matmul(ps[4][0:1, 0:512], onescb_s[:, 0:1], lnb_s[:, :],
                               start=(ot == 0), stop=(ot == NHC - 1)) if em.on("t") else None
                em.inc("t", ins, SM["s_mm"])
                ins = t.matmul(ps[5][0:1, 0:512], onescb_s[:, 0:1], ln2b_s[:, :],
                               start=(ot == 0), stop=(ot == NHC - 1)) if em.on("t") else None
                mmv = em.inc("t", ins, SM["s_mm"])
                em.wait("v", SM["s_mm"], mmv)
                em.wait("s", SM["s_mm"], mmv)
            em.wait("v", SM["s_mm"])
            ins = v.tensor_scalar(mean_s[0:1, :], ps[4][0:1, 0:512], 1.0 / HID, 0.0,
                                  ALU.mult, ALU.add) if em.on("v") else None
            em.inc("v", ins, SM["s_dve"])
            ins = v.tensor_scalar(ex2_s[0:1, :], ps[5][0:1, 0:512], 1.0 / HID, 0.0,
                                  ALU.mult, ALU.add) if em.on("v") else None
            em.inc("v", ins, SM["s_dve"])
            ins = v.tensor_mul(var_s[0:1, :], mean_s[0:1, :], mean_s[0:1, :]) if em.on("v") else None
            em.inc("v", ins, SM["s_dve"])
            ins = v.tensor_sub(var_s[0:1, :], ex2_s[0:1, :], var_s[0:1, :]) if em.on("v") else None
            dvv = em.inc("v", ins, SM["s_dve"])
            ins = v.tensor_scalar(var_s[0:1, :], var_s[0:1, :], float(EPS), 0.0,
                                  ALU.add, ALU.add) if em.on("v") else None
            dvv = em.inc("v", ins, SM["s_dve"])
            em.wait("s", SM["s_dve"], dvv)
            ins = s.activation(sq_s[0:1, :], var_s[0:1, :], AF.Sqrt) if em.on("s") else None
            sa = em.inc("s", ins, SM["s_act"])
            em.wait("v", SM["s_act"], sa)
            ins = v.reciprocal(rstd_s[0:1, :], sq_s[0:1, :]) if em.on("v") else None
            dvv = em.inc("v", ins, SM["s_dve"])
            em.wait("t", SM["s_dve"], dvv)
            ins = t.matmul(ps[4][:, 0:512], onesrF_s[:, :], mean_s[0:1, :], start=True, stop=True) if em.on("t") else None
            em.inc("t", ins, SM["s_mm"])
            ins = t.matmul(ps[5][:, 0:512], onesrF_s[:, :], rstd_s[0:1, :], start=True, stop=True) if em.on("t") else None
            mmv = em.inc("t", ins, SM["s_mm"])
            em.wait("s", SM["s_mm"], mmv)
            ins = s.copy(meanB_s[:, :], ps[4][:, 0:512]) if em.on("s") else None
            em.inc("s", ins, SM["s_act"])
            ins = s.copy(rstdB_s[:, :], ps[5][:, 0:512]) if em.on("s") else None
            sa = em.inc("s", ins, SM["s_act"])
            em.wait("v", SM["s_act"], sa)
            ins = None
            for ot in range(NHC):
                sl = src_f32[:, ot * 512:(ot + 1) * 512]
                ins = v.tensor_sub(tmp_s[:, :], sl, meanB_s[:, :]) if em.on("v") else None
                em.inc("v", ins, SM["s_dve"])
                ins = v.tensor_mul(dst_bf[:, ot * dst_cols_per_ot + t_off: ot * dst_cols_per_ot + t_off + 512],
                                   tmp_s[:, :], rstdB_s[:, :]) if em.on("v") else None
                em.inc("v", ins, SM["s_dve"])

        def schedule(em: Emit):
            state.clear()
            SM = sems
            t = em.h.get("t")
            v = em.h.get("v")
            s = em.h.get("s")
            g = em.h.get("g")
            q = em.h.get("q")

            ins = q.dma_start(out=onesrF_s[:, :], in_=onesr_d) if em.on("q") else None
            em.inc("q", ins, SM["s_ld"], 16)
            em.wait("v", SM["s_ld"])
            ins = v.memset(onescb_s[:, :], 1.0) if em.on("v") else None
            em.inc("v", ins, SM["s_dve"])

            for tc2 in range(2):
                src3 = attnT[:, tc2 * 512:(tc2 + 1) * 512].rearrange("(hc p) t -> p hc t", p=128)
                ins = q.dma_start(out=at_s[:, :].rearrange("p (hc t) -> p hc t", t=512),
                                  in_=src3) if em.on("q") else None
                at_ld = em.inc("q", ins, SM["s_x"], 16)
                for og in range(4):
                    for hc in range(NHC):
                        wb = hc % 2
                        em.wait("q", SM["s_act"], state.get(("wofree", wb), 0))
                        em.wait("q", SM["s_mm"], state.get(("scrbfree", wb), 0))
                        ins = q.dma_start(out=scrf[wb][:, :],
                                          in_=woT[hc * 128:(hc + 1) * 128, og * 512:(og + 1) * 512]) if em.on("q") else None
                        lv = em.inc("q", ins, SM["s_w%d" % wb], 16)
                        em.wait("s", SM["s_w%d" % wb], lv)
                        dstw = scrb[:, 8192 + wb * 512: 8192 + (wb + 1) * 512]
                        ins = s.activation(dstw, scrf[wb][:, :], AF.Copy) if em.on("s") else None
                        wcast = em.inc("s", ins, SM["s_act"])
                        state[("wofree", wb)] = wcast
                        em.wait("t", SM["s_act"], wcast)
                        em.wait("t", SM["s_x"], at_ld)
                        ins = None
                        for oi in range(4):
                            ins = t.matmul(ps[oi][:, 0:512],
                                           scrb[:, 8192 + wb * 512 + oi * 128: 8192 + wb * 512 + oi * 128 + 128],
                                           at_s[:, hc * 512:(hc + 1) * 512],
                                           start=(hc == 0), stop=(hc == NHC - 1)) if em.on("t") else None
                        state[("scrbfree", wb)] = em.inc("t", ins, SM["s_mm"])
                    womm = SM["s_mm"].n
                    for oi in range(4):
                        ot = og * 4 + oi
                        xb2 = oi % 2
                        ins = q.dma_start(out=xf2[xb2][:, :],
                                          in_=xTs[ot * 128:(ot + 1) * 128, tc2 * 512:(tc2 + 1) * 512]) if em.on("q") else None
                        lv = em.inc("q", ins, SM["s_x%d" % xb2], 16)
                        em.wait("v", SM["s_x%d" % xb2], lv)
                        em.wait("v", SM["s_mm"], womm)
                        em.wait("v", SM["s_act"])
                        ins = v.tensor_add(acc_s[:, ot * 512:(ot + 1) * 512], ps[oi][:, 0:512],
                                           xf2[xb2][:, :]) if em.on("v") else None
                        dvv = em.inc("v", ins, SM["s_dve"])
                        em.wait("q", SM["s_dve"], dvv)
                        em.wait("t", SM["s_dve"], dvv)
                layernorm(em, tc2, acc_s, xfb_s, 1024, tc2 * 512)
                ln_done = SM["s_dve"].n
                em.wait("q", SM["s_dve"], ln_done)

            xfb_ready = SM["s_dve"].n
            em.wait("q", SM["s_mm"])
            for j in range(NIT):
                wb = j % 2
                base = wb * 4096
                em.wait("q", SM["s_mm"], state.get(("gufree", wb), 0))
                src3 = wgT[:, j * 128:(j + 1) * 128].rearrange("(hc p) i -> p hc i", p=128)
                ins = q.dma_start(out=scrb[:, base:base + 2048].rearrange("p (hc i) -> p hc i", i=128),
                                  in_=src3) if em.on("q") else None
                em.inc("q", ins, SM["s_w%d" % wb], 16)
                src3 = wuT[:, j * 128:(j + 1) * 128].rearrange("(hc p) i -> p hc i", p=128)
                ins = q.dma_start(out=scrb[:, base + 2048:base + 4096].rearrange("p (hc i) -> p hc i", i=128),
                                  in_=src3) if em.on("q") else None
                lv = em.inc("q", ins, SM["s_w%d" % wb], 16)
                em.wait("t", SM["s_w%d" % wb], lv)
                em.wait("t", SM["s_dve"], xfb_ready)
                for th in range(2):
                    em.wait("t", SM["s_dve"], state.get(("hfree",), 0))
                    ins = None
                    for hc in range(NHC):
                        ins = t.matmul(ps[0][:, 0:512], scrb[:, base + hc * 128: base + hc * 128 + 128],
                                       xfb_s[:, hc * 1024 + th * 512: hc * 1024 + th * 512 + 512],
                                       start=(hc == 0), stop=(hc == NHC - 1)) if em.on("t") else None
                    em.inc("t", ins, SM["s_mm"])
                    ins = None
                    for hc in range(NHC):
                        ins = t.matmul(ps[1][:, 0:512], scrb[:, base + 2048 + hc * 128: base + 2048 + hc * 128 + 128],
                                       xfb_s[:, hc * 1024 + th * 512: hc * 1024 + th * 512 + 512],
                                       start=(hc == 0), stop=(hc == NHC - 1)) if em.on("t") else None
                    gumm = em.inc("t", ins, SM["s_mm"])
                    state[("gufree", wb)] = gumm
                    em.wait("s", SM["s_mm"], gumm)
                    ins = s.activation(sg_s[:, :], ps[0][:, 0:512], AF.Silu) if em.on("s") else None
                    sa = em.inc("s", ins, SM["s_act"])
                    em.wait("v", SM["s_act"], sa)
                    ins = v.tensor_mul(hh_s[:, j * 1024 + th * 512: j * 1024 + th * 512 + 512],
                                       sg_s[:, :], ps[1][:, 0:512]) if em.on("v") else None
                    hdv = em.inc("v", ins, SM["s_dve"])
                    state[("hfree",)] = hdv

            em.wait("q", SM["s_mm"])
            for th in range(2):
                for ot in range(NHC):
                    wb = ot % 2
                    base = wb * 5632
                    em.wait("q", SM["s_mm"], state.get(("wdfree", wb), 0))
                    src3 = wdT[:, ot * 128:(ot + 1) * 128].rearrange("(sl p) o -> p sl o", p=128)
                    ins = q.dma_start(out=scrb[:, base:base + NIT * 128].rearrange("p (sl o) -> p sl o", o=128),
                                      in_=src3) if em.on("q") else None
                    lv = em.inc("q", ins, SM["s_w%d" % wb], 16)
                    em.wait("t", SM["s_w%d" % wb], lv)
                    em.wait("t", SM["s_dve"])
                    par = ot % 2
                    em.wait("t", SM["s_dve"], state.get(("dacc", par), 0))
                    ins = None
                    for j in range(NIT):
                        ins = t.matmul(ps[2 + par][:, 0:512], scrb[:, base + j * 128: base + j * 128 + 128],
                                       hh_s[:, j * 1024 + th * 512: j * 1024 + th * 512 + 512],
                                       start=(j == 0), stop=(j == NIT - 1)) if em.on("t") else None
                    dmm = em.inc("t", ins, SM["s_mm"])
                    state[("wdfree", wb)] = dmm
                    em.wait("v", SM["s_mm"], dmm)
                    em.wait("v", SM["s_act"])
                    em.wait("v", SM["s_st"])
                    ins = v.tensor_add(acc_s[:, ot * 512:(ot + 1) * 512], ps[2 + par][:, 0:512],
                                       xfb_s[:, ot * 1024 + th * 512: ot * 1024 + th * 512 + 512]) if em.on("v") else None
                    state[("dacc", par)] = em.inc("v", ins, SM["s_dve"])
                layernorm_ln2(em, th)

            em.wait("q", SM["s_st"])
            em.wait("g", SM["s_st"])

        def layernorm_ln2(em, th):
            SM = sems
            t = em.h.get("t")
            v = em.h.get("v")
            s = em.h.get("s")
            q = em.h.get("q")
            ins = None
            for ot in range(NHC):
                sl = acc_s[:, ot * 512:(ot + 1) * 512]
                ins = v.tensor_copy(lnb_s[:, :], sl) if em.on("v") else None
                em.inc("v", ins, SM["s_dve"])
                em.wait("s", SM["s_dve"])
                ins = s.activation(ln2b_s[:, :], sl, AF.Square) if em.on("s") else None
                sa = em.inc("s", ins, SM["s_act"])
                dvv = SM["s_dve"].n
                em.wait("t", SM["s_dve"], dvv)
                em.wait("t", SM["s_act"], sa)
                ins = t.matmul(ps[4][0:1, 0:512], onescb_s[:, 0:1], lnb_s[:, :],
                               start=(ot == 0), stop=(ot == NHC - 1)) if em.on("t") else None
                em.inc("t", ins, SM["s_mm"])
                ins = t.matmul(ps[5][0:1, 0:512], onescb_s[:, 0:1], ln2b_s[:, :],
                               start=(ot == 0), stop=(ot == NHC - 1)) if em.on("t") else None
                mmv = em.inc("t", ins, SM["s_mm"])
                em.wait("v", SM["s_mm"], mmv)
                em.wait("s", SM["s_mm"], mmv)
            em.wait("v", SM["s_mm"])
            ins = v.tensor_scalar(mean_s[0:1, :], ps[4][0:1, 0:512], 1.0 / HID, 0.0,
                                  ALU.mult, ALU.add) if em.on("v") else None
            em.inc("v", ins, SM["s_dve"])
            ins = v.tensor_scalar(ex2_s[0:1, :], ps[5][0:1, 0:512], 1.0 / HID, 0.0,
                                  ALU.mult, ALU.add) if em.on("v") else None
            em.inc("v", ins, SM["s_dve"])
            ins = v.tensor_mul(var_s[0:1, :], mean_s[0:1, :], mean_s[0:1, :]) if em.on("v") else None
            em.inc("v", ins, SM["s_dve"])
            ins = v.tensor_sub(var_s[0:1, :], ex2_s[0:1, :], var_s[0:1, :]) if em.on("v") else None
            dvv = em.inc("v", ins, SM["s_dve"])
            ins = v.tensor_scalar(var_s[0:1, :], var_s[0:1, :], float(EPS), 0.0,
                                  ALU.add, ALU.add) if em.on("v") else None
            dvv = em.inc("v", ins, SM["s_dve"])
            em.wait("s", SM["s_dve"], dvv)
            ins = s.activation(sq_s[0:1, :], var_s[0:1, :], AF.Sqrt) if em.on("s") else None
            sa = em.inc("s", ins, SM["s_act"])
            em.wait("v", SM["s_act"], sa)
            ins = v.reciprocal(rstd_s[0:1, :], sq_s[0:1, :]) if em.on("v") else None
            dvv = em.inc("v", ins, SM["s_dve"])
            em.wait("t", SM["s_dve"], dvv)
            ins = t.matmul(ps[4][:, 0:512], onesrF_s[:, :], mean_s[0:1, :], start=True, stop=True) if em.on("t") else None
            em.inc("t", ins, SM["s_mm"])
            ins = t.matmul(ps[5][:, 0:512], onesrF_s[:, :], rstd_s[0:1, :], start=True, stop=True) if em.on("t") else None
            mmv = em.inc("t", ins, SM["s_mm"])
            em.wait("s", SM["s_mm"], mmv)
            ins = s.copy(meanB_s[:, :], ps[4][:, 0:512]) if em.on("s") else None
            em.inc("s", ins, SM["s_act"])
            ins = s.copy(rstdB_s[:, :], ps[5][:, 0:512]) if em.on("s") else None
            sa = em.inc("s", ins, SM["s_act"])
            em.wait("v", SM["s_act"], sa)
            for ot in range(NHC):
                sl = acc_s[:, ot * 512:(ot + 1) * 512]
                ins = v.tensor_sub(tmp_s[:, :], sl, meanB_s[:, :]) if em.on("v") else None
                em.inc("v", ins, SM["s_dve"])
                ins = v.tensor_mul(sl, tmp_s[:, :], rstdB_s[:, :]) if em.on("v") else None
                dvv = em.inc("v", ins, SM["s_dve"])
                em.wait("q", SM["s_dve"], dvv)
                ins = q.dma_start(out=outT[ot * 128:(ot + 1) * 128, th * 512:(th + 1) * 512],
                                  in_=sl) if em.on("q") else None
                em.inc("q", ins, SM["s_st"], 16)

        @block.tensor
        def _(tt):
            schedule(Emit("t", {"t": tt, "v": None, "s": None, "g": None, "q": None}, sems))
        for sm in sems.values():
            sm.n = 0

        @block.vector
        def _(vv):
            schedule(Emit("v", {"t": None, "v": SelfSync(vv, sems["s_dve"]), "s": None, "g": None, "q": None}, sems))
        for sm in sems.values():
            sm.n = 0

        @block.scalar
        def _(ss):
            schedule(Emit("s", {"t": None, "v": None, "s": SelfSync(ss, sems["s_act"]), "g": None, "q": None}, sems))
        for sm in sems.values():
            sm.n = 0

        @block.gpsimd
        def _(gg):
            schedule(Emit("g", {"t": None, "v": None, "s": None, "g": gg, "q": None}, sems))
        for sm in sems.values():
            sm.n = 0

        @block.sync
        def _(qq):
            schedule(Emit("q", {"t": None, "v": None, "s": None, "g": None, "q": qq}, sems))

    return nc


_cache = {}


def kernel(L0_attn_hidden_states, L0_attn_Wq, L0_attn_Wk, L0_attn_Wv, L0_attn_Wo,
           L0_ffn_lut, L0_ffn_gate_idx, L0_ffn_up_idx, L0_ffn_down_idx):
    x = np.asarray(L0_attn_hidden_states)
    Wq = np.asarray(L0_attn_Wq)
    Wk = np.asarray(L0_attn_Wk)
    Wv = np.asarray(L0_attn_Wv)
    Wo = np.asarray(L0_attn_Wo)
    lut = np.asarray(L0_ffn_lut).astype(np.float32, copy=False)
    gidx = np.asarray(L0_ffn_gate_idx)
    uidx = np.asarray(L0_ffn_up_idx)
    didx = np.asarray(L0_ffn_down_idx)

    consts = build_consts()
    ac = np.ascontiguousarray

    xT_b = [ac(x[b].T) for b in range(2)]
    gidxT = ac(gidx.T)
    uidxT = ac(uidx.T)
    didxT = ac(didx.T)

    if "l1" not in _cache:
        _cache["l1"] = build_l1()
    nc1 = _cache["l1"]
    in_maps = []
    for c in range(8):
        b, gq = c // 4, c % 4
        m = dict(xT=xT_b[b],
                 wqT=ac(Wq[512 * gq:512 * gq + 512, :].T),
                 wkT=ac(Wk[512 * gq:512 * gq + 512, :].T),
                 wvT=ac(Wv[512 * gq:512 * gq + 512, :].T),
                 gidxT=ac(gidxT[:, 688 * c:688 * c + 688]),
                 uidxT=ac(uidxT[:, 688 * c:688 * c + 688]),
                 didxT=ac(didxT[688 * c:688 * c + 688, :]),
                 lut16=lut.reshape(1, 16))
        for k in L1_CONSTS:
            m[k] = consts[k]
        in_maps.append(m)
    res1 = run_bass_kernel_spmd(nc1, in_maps, core_ids=list(range(8)))

    attnT_full = [np.concatenate([res1.results[b * 4 + gq]["attnT_o"] for gq in range(4)], axis=0)
                  for b in range(2)]
    wgT_full = np.concatenate([res1.results[c]["wg_o"] for c in range(8)], axis=1)
    wuT_full = np.concatenate([res1.results[c]["wu_o"] for c in range(8)], axis=1)
    wdT_full = np.concatenate([res1.results[c]["wd_o"] for c in range(8)], axis=0)

    if "l2" not in _cache:
        _cache["l2"] = build_l2()
    nc2 = _cache["l2"]
    woT = ac(Wo.T)
    in_maps2 = []
    for c in range(8):
        b, gq = c // 4, c % 4
        t0 = gq * 1024
        m = dict(attnT=ac(attnT_full[b][:, t0:t0 + 1024]),
                 xTs=ac(xT_b[b][:, t0:t0 + 1024]),
                 woT=woT, wgT=wgT_full, wuT=wuT_full, wdT=wdT_full,
                 onesrC=consts["onesrC"])
        in_maps2.append(m)
    res2 = run_bass_kernel_spmd(nc2, in_maps2, core_ids=list(range(8)))

    out = np.empty((2, S, HID), dtype=np.float32)
    for c in range(8):
        b, gq = c // 4, c % 4
        t0 = gq * 1024
        out[b, t0:t0 + 1024, :] = res2.results[c]["outT"].T
    return out
